# version 07 (final)
# baseline (speedup 1.0000x reference)
import numpy as np
import concourse.bass as bass
import concourse.mybir as mybir
from concourse.bass_utils import run_bass_kernel_spmd
from contextlib import ExitStack

F32 = mybir.dt.float32
BF16 = mybir.dt.bfloat16
AF = mybir.ActivationFunctionType
ALU = mybir.AluOpType

B, T, C = 2, 4096, 2048
NH, LC = 16, 128
E, NI = 32, 768
NOWN = 1024
EPS = 1e-5
ROPE_THETA = 10000.0
EXP_SHIFT = -24.0
LN_HD = float(np.log(1.0 / np.sqrt(128.0)))

OWN = [sorted([s, 7 - s, 8 + s, 15 - s, 16 + s, 23 - s, 24 + s, 31 - s])
       for s in range(4)]


def _rope_tables(npos):
    half = LC // 2
    pos = np.arange(npos, dtype=np.float64)
    inv = ROPE_THETA ** (-np.arange(half, dtype=np.float64) / half)
    ang = pos[None, :] * inv[:, None]
    cos = np.repeat(np.cos(ang), 2, axis=0)
    sin = np.repeat(np.sin(ang), 2, axis=0)
    return cos.astype(np.float32), sin.astype(np.float32)


def _perm_matrix_T():
    P = np.zeros((LC, LC), dtype=np.float32)
    for i in range(LC // 2):
        P[2 * i, 2 * i + 1] = -1.0
        P[2 * i + 1, 2 * i] = 1.0
    return np.ascontiguousarray(P.T)


class Sem:
    def __init__(self, nc, name):
        self.h = nc.alloc_semaphore(name)
        self.v = 0

    def inc(self, inst, n=16):
        inst.then_inc(self.h, n)
        self.v += n


def build_program(part):
    hasA = "A" in part
    hasB = "B" in part
    nc = bass.Bass("TRN2", target_bir_lowering=False)

    if hasA:
        d_x_own = nc.dram_tensor("x_own", [NOWN, C], F32, kind="ExternalInput").ap()
        d_x_batch = nc.dram_tensor("x_batch", [T, C], F32, kind="ExternalInput").ap()
        d_Wlat = nc.dram_tensor("Wlat", [C, LC], F32, kind="ExternalInput").ap()
        d_blat = nc.dram_tensor("blat", [LC, 1], F32, kind="ExternalInput").ap()
        d_Wd = nc.dram_tensor("Wd", [C, C], F32, kind="ExternalInput").ap()
        d_bd = nc.dram_tensor("bd", [LC, NH], F32, kind="ExternalInput").ap()
        d_Wproj = nc.dram_tensor("Wproj", [C, C], F32, kind="ExternalInput").ap()
        d_bproj = nc.dram_tensor("bproj", [128, C], F32, kind="ExternalInput").ap()
        d_WqT = nc.dram_tensor("WqT", [C, C], F32, kind="ExternalInput").ap()
        d_WkT = nc.dram_tensor("WkT", [C, C], F32, kind="ExternalInput").ap()
        d_WvT = nc.dram_tensor("WvT", [C, C], F32, kind="ExternalInput").ap()
        d_gamma = nc.dram_tensor("gamma", [128, C], F32, kind="ExternalInput").ap()
        d_beta = nc.dram_tensor("beta", [128, C], F32, kind="ExternalInput").ap()
        d_cosq = nc.dram_tensor("cosq", [LC, NOWN], F32, kind="ExternalInput").ap()
        d_sinq = nc.dram_tensor("sinq", [LC, NOWN], F32, kind="ExternalInput").ap()
        d_cosk = nc.dram_tensor("cosk", [LC, T], F32, kind="ExternalInput").ap()
        d_sink = nc.dram_tensor("sink", [LC, T], F32, kind="ExternalInput").ap()
    if hasB:
        d_x2in = nc.dram_tensor("x2in", [NOWN, C], F32, kind="ExternalInput").ap()
        d_Kf = nc.dram_tensor("Kf", [T, C], BF16, kind="ExternalInput").ap()
        d_Vf = nc.dram_tensor("Vf", [T, C], BF16, kind="ExternalInput").ap()
        d_qf = nc.dram_tensor("qf", [NOWN, C], BF16, kind="ExternalInput").ap()
        d_WoT = nc.dram_tensor("WoT", [C, C], F32, kind="ExternalInput").ap()

    d_gwT = nc.dram_tensor("gwT", [C, E], F32, kind="ExternalInput").ap()
    d_wg = nc.dram_tensor("wg", [E, C, NI], F32, kind="ExternalInput").ap()
    d_wu = nc.dram_tensor("wu", [E, C, NI], F32, kind="ExternalInput").ap()
    d_wdn = nc.dram_tensor("wdn", [E, NI, C], F32, kind="ExternalInput").ap()
    d_qpos = nc.dram_tensor("qpos", [128, NOWN], F32, kind="ExternalInput").ap()
    d_kpos = nc.dram_tensor("kpos", [128, T // 128], F32, kind="ExternalInput").ap()

    if hasA:
        d_x2 = nc.dram_tensor("x2", [NOWN, C], F32, kind="ExternalOutput").ap()
        d_K = nc.dram_tensor("K", [NOWN, C], BF16, kind="ExternalOutput").ap()
        d_V = nc.dram_tensor("V", [NOWN, C], BF16, kind="ExternalOutput").ap()
        d_q = nc.dram_tensor("q", [NOWN, C], BF16, kind="ExternalOutput").ap()
    if hasB:
        d_out = nc.dram_tensor("out", [NOWN, C], F32, kind="ExternalOutput").ap()

    d_xns = nc.dram_tensor("xn_s", [NOWN, C], BF16).ap()
    if hasA:
        d_xn = nc.dram_tensor("xn_all", [T, C], BF16).ap()
        d_xno = nc.dram_tensor("xn_own", [NOWN, C], BF16).ap()
        d_xr = nc.dram_tensor("x_resid", [NOWN, C], F32).ap()
        d_xns2 = nc.dram_tensor("xn_s2", [NOWN, C], BF16).ap()
    if hasB:
        d_x3 = nc.dram_tensor("x3_resid", [NOWN, C], F32).ap()

    c_PT = nc.inline_tensor(_perm_matrix_T(), "PTc")
    c_onescol = nc.inline_tensor(np.ones((128, 1), dtype=np.float32), "onescolc")
    c_onesrow = nc.inline_tensor(np.ones((1, 128), dtype=np.float32), "onesrowc")
    c_ident = nc.inline_tensor(np.eye(128, dtype=np.float32), "identc")

    sb_kpos = nc.alloc_sbuf_tensor("kpos_sb", [128, T // 128], F32)
    sb_gwT = nc.alloc_sbuf_tensor("gwT_sb", [128, C // 128, E], BF16)
    sb_onescol = nc.alloc_sbuf_tensor("ones_colb", [128, 1], BF16)
    sb_onesrow = nc.alloc_sbuf_tensor("ones_rowb", [1, 128], BF16)
    sb_eshift = nc.alloc_sbuf_tensor("eshift_sb", [128, 1], F32)
    sb_zero = nc.alloc_sbuf_tensor("zero_sb", [128, 1], F32)
    sb_lnhd = nc.alloc_sbuf_tensor("lnhd_sb", [128, 1], F32)
    if hasA:
        sb_bd = nc.alloc_sbuf_tensor("bd_sb", [128, NH], F32)
        sb_blat = nc.alloc_sbuf_tensor("blat_sb", [128, 1], F32)
        sb_PT = nc.alloc_sbuf_tensor("PT_bf", [128, 128], BF16)
        sb_ident = nc.alloc_sbuf_tensor("ident_bf", [128, 128], BF16)

    sems = {}

    def S(name):
        if name not in sems:
            sems[name] = Sem(nc, name)
        return sems[name]

    _clrn = [0]

    def free_sems():
        if sems:
            handles = [s.h for s in sems.values()]
            _clrn[0] += 1
            with nc.Block(f"clr{_clrn[0]}") as blk:
                @blk.gpsimd
                def _(g):
                    nc.clear_and_free_semaphores(handles)
            sems.clear()

    def emit_ln(tag, src, srow0, dst, drow0, n, xacc=None):
        with ExitStack() as ctx:
            if xacc is None:
                xt = ctx.enter_context(nc.sbuf_tensor(tag + "_x", [128, 2, C], F32))
            xnt = ctx.enter_context(nc.sbuf_tensor(tag + "_xn", [128, 2, C], BF16))
            st_ = ctx.enter_context(nc.sbuf_tensor(tag + "_st", [128, 2, 24], F32))
            ag = ctx.enter_context(nc.sbuf_tensor(tag + "_ag", [128, 2, 4], F32))
            with nc.Block(tag) as blk:
                xld = S(tag + "_xld"); vst = S(tag + "_vst"); rst = S(tag + "_rst")
                xn2 = S(tag + "_xn2"); xnst = S(tag + "_xnst")

                def xtile(tt):
                    return xacc[:, tt, :] if xacc is not None else xt[:, tt % 2, :]

                @blk.sync
                def _(sy):
                    for tt in range(n):
                        if xacc is None and tt >= 2:
                            sy.wait_ge(xn2.h, tt - 1)
                        xld.inc(sy.dma_start(out=xtile(tt),
                                             in_=src[srow0 + tt * 128:srow0 + (tt + 1) * 128, :]))

                @blk.vector
                def _(v):
                    vf = S(tag + "_vf")
                    for tt in range(n):
                        p = tt % 2
                        v.wait_ge(xld.h, 16 * (tt + 1))
                        x = xtile(tt)
                        for j in range(4):
                            ins = v.bn_stats(st_[:, p, j * 6:(j + 1) * 6],
                                             x[:, j * 512:(j + 1) * 512])
                        ins.then_inc(vf.h, 1); vf.v += 1
                        v.wait_ge(vf.h, vf.v)
                        v.bn_aggr(ag[:, p, 0:2], st_[:, p, 0:24]).then_inc(vf.h, 1)
                        vf.v += 1
                        v.wait_ge(vf.h, vf.v)
                        v.tensor_scalar_add(ag[:, p, 2:3], ag[:, p, 1:2], EPS) \
                            .then_inc(vst.h, 1)
                        vst.v += 1
                        v.wait_ge(rst.h, tt + 1)
                        if tt >= 2:
                            v.wait_ge(xnst.h, 16 * (tt - 1))
                        v.tensor_scalar(xnt[:, p, :], x, ag[:, p, 0:1], ag[:, p, 3:4],
                                        ALU.subtract, ALU.mult).then_inc(xn2.h, 1)
                        xn2.v += 1

                @blk.scalar
                def _(sc):
                    sf = S(tag + "_sf")
                    for tt in range(n):
                        p = tt % 2
                        sc.wait_ge(vst.h, tt + 1)
                        sc.activation(ag[:, p, 3:4], ag[:, p, 2:3], AF.Ln).then_inc(sf.h, 1)
                        sf.v += 1
                        sc.wait_ge(sf.h, sf.v)
                        sc.activation(ag[:, p, 3:4], ag[:, p, 3:4], AF.Exp, scale=-0.5) \
                            .then_inc(rst.h, 1)
                        rst.v += 1

                @blk.gpsimd
                def _(g):
                    for tt in range(n):
                        g.wait_ge(xn2.h, tt + 1)
                        xnst.inc(g.dma_start(out=dst[drow0 + tt * 128:drow0 + (tt + 1) * 128, :],
                                             in_=xnt[:, tt % 2, :]))
                    g.wait_ge(xnst.h, xnst.v)

    def emit_moe(tag, xacc, xn_dram, xnrow0):
        with ExitStack() as ctx:
            xnT = ctx.enter_context(nc.sbuf_tensor(tag + "_xnT", [128, 16, 1024], BF16))
            wgu = ctx.enter_context(nc.sbuf_tensor(tag + "_wgu", [128, 2, 2, 16, 128], BF16))
            wdn = ctx.enter_context(nc.sbuf_tensor(tag + "_wdn", [128, 2, 6, C], BF16))
            hbuf = ctx.enter_context(nc.sbuf_tensor(tag + "_h", [128, 2, 6, 1024], BF16))
            sg = ctx.enter_context(nc.sbuf_tensor(tag + "_sg", [128, 2, 512], BF16))
            wnorm = ctx.enter_context(nc.sbuf_tensor(tag + "_wn", [128, 8, E], F32))
            rsc = ctx.enter_context(nc.sbuf_tensor(tag + "_rsc", [128, 6, E], F32))

            ps_g = ctx.enter_context(nc.psum_tensor(tag + "_pg", [128, 2, 512], F32))
            ps_u = ctx.enter_context(nc.psum_tensor(tag + "_pu", [128, 2, 512], F32))
            ps_dn = ctx.enter_context(nc.psum_tensor(tag + "_pd", [128, 2, 512], F32))
            ps_l = ctx.enter_context(nc.psum_tensor(tag + "_pl", [128, 32], F32))

            with nc.Block(tag) as blk:
                tl = S(tag + "_tl"); rmm = S(tag + "_rmm"); ract = S(tag + "_ract")
                rdve = S(tag + "_rdve"); rdone = S(tag + "_rdone")
                wld = S(tag + "_wld"); dld = S(tag + "_dld")
                gmm = S(tag + "_gmm"); umm = S(tag + "_umm"); sil = S(tag + "_sil")
                hh = S(tag + "_hh"); dmm = S(tag + "_dmm"); ev = S(tag + "_ev")

                @blk.sync
                def _(sy):
                    for xf in range(16):
                        tl.inc(sy.dma_start(out=xnT[:, xf, :],
                                            in_=xn_dram[xnrow0:xnrow0 + 1024,
                                                        xf * 128:(xf + 1) * 128],
                                            transpose=True))

                @blk.gpsimd
                def _(g):
                    for e in range(E):
                        for i in range(6):
                            k = e * 6 + i
                            if k >= 2:
                                g.wait_ge(gmm.h, 2 * k - 2)
                                g.wait_ge(umm.h, 2 * k - 2)
                            wld.inc(g.dma_start(
                                out=wgu[:, k % 2, 0, :, :],
                                in_=d_wg[e][:, i * 128:(i + 1) * 128].rearrange(
                                    "(t p) i -> p t i", p=128)))
                            wld.inc(g.dma_start(
                                out=wgu[:, k % 2, 1, :, :],
                                in_=d_wu[e][:, i * 128:(i + 1) * 128].rearrange(
                                    "(t p) i -> p t i", p=128)))
                        if e >= 2:
                            g.wait_ge(dmm.h, 32 * (e - 1))
                        dld.inc(g.dma_start(
                            out=wdn[:, e % 2, :, :],
                            in_=d_wdn[e].rearrange("(i p) c -> p i c", p=128)))

                @blk.tensor
                def _(t):
                    t.wait_ge(tl.h, 16 * 16)
                    for tt in range(8):
                        if tt >= 1:
                            t.wait_ge(ract.h, tt)
                        for xf in range(16):
                            ins = t.matmul(ps_l[:, :], xnT[:, xf, tt * 128:(tt + 1) * 128],
                                           sb_gwT[:, xf, :], start=(xf == 0), stop=(xf == 15))
                        ins.then_inc(rmm.h, 1); rmm.v += 1
                    for e in range(E):
                        for i in range(6):
                            k = e * 6 + i
                            t.wait_ge(wld.h, 32 * (k + 1))
                            for h2 in range(2):
                                gi = 2 * k + h2
                                if gi >= 2:
                                    t.wait_ge(sil.h, gi - 1)
                                for xf in range(16):
                                    ins = t.matmul(ps_g[:, h2, :], wgu[:, k % 2, 0, xf, :],
                                                   xnT[:, xf, h2 * 512:(h2 + 1) * 512],
                                                   start=(xf == 0), stop=(xf == 15))
                                ins.then_inc(gmm.h, 1); gmm.v += 1
                                if gi >= 2:
                                    t.wait_ge(hh.h, gi - 1)
                                for xf in range(16):
                                    ins = t.matmul(ps_u[:, h2, :], wgu[:, k % 2, 1, xf, :],
                                                   xnT[:, xf, h2 * 512:(h2 + 1) * 512],
                                                   start=(xf == 0), stop=(xf == 15))
                                ins.then_inc(umm.h, 1); umm.v += 1
                        t.wait_ge(hh.h, 12 * (e + 1))
                        t.wait_ge(dld.h, 16 * (e + 1))
                        t.wait_ge(rdone.h, 8)
                        for nch in range(4):
                            for tt in range(8):
                                di = e * 32 + nch * 8 + tt
                                if di >= 2:
                                    t.wait_ge(ev.h, di - 1)
                                for i in range(6):
                                    ins = t.matmul(ps_dn[:, di % 2, :],
                                                   hbuf[:, e % 2, i, tt * 128:(tt + 1) * 128],
                                                   wdn[:, e % 2, i, nch * 512:(nch + 1) * 512],
                                                   start=(i == 0), stop=(i == 5))
                                ins.then_inc(dmm.h, 1); dmm.v += 1

                @blk.scalar
                def _(sc):
                    for tt in range(8):
                        sc.wait_ge(rdve.h, 2 * tt + 1)
                        sc.activation(rsc[:, 1, 0:32], ps_l[:, :], AF.Exp,
                                      bias=rsc[:, 0, tt:tt + 1],
                                      accum_out=rsc[:, 2, tt:tt + 1]).then_inc(ract.h, 1)
                        ract.v += 1
                        sc.wait_ge(rdve.h, 2 * tt + 2)
                    for e in range(E):
                        for i in range(6):
                            for h2 in range(2):
                                gi = 2 * (e * 6 + i) + h2
                                sc.wait_ge(gmm.h, gi + 1)
                                if gi >= 2:
                                    sc.wait_ge(hh.h, gi - 1)
                                sc.activation(sg[:, h2, :], ps_g[:, h2, :], AF.Silu) \
                                    .then_inc(sil.h, 1)
                                sil.v += 1

                @blk.vector
                def _(v):
                    vf = S(tag + "_vf")

                    def fence(ins):
                        ins.then_inc(vf.h, 1)
                        vf.v += 1
                        v.wait_ge(vf.h, vf.v)

                    for tt in range(8):
                        v.wait_ge(rmm.h, tt + 1)
                        fence(v.reduce_max(rsc[:, 3, tt:tt + 1], ps_l[:, :],
                                           axis=mybir.AxisListType.X))
                        v.tensor_scalar_mul(rsc[:, 0, tt:tt + 1], rsc[:, 3, tt:tt + 1], -1.0) \
                            .then_inc(rdve.h, 1)
                        rdve.v += 1
                        v.wait_ge(ract.h, tt + 1)
                        fence(v.reduce_sum(rsc[:, 2, tt:tt + 1], rsc[:, 1, 0:32],
                                           axis=mybir.AxisListType.X))
                        fence(v.reciprocal(rsc[:, 3, tt:tt + 1], rsc[:, 2, tt:tt + 1]))
                        fence(v.tensor_scalar_mul(wnorm[:, tt, :], rsc[:, 1, 0:32],
                                                  rsc[:, 3, tt:tt + 1]))
                        v.engine_nop().then_inc(rdve.h, 1)
                        rdve.v += 1
                        fence(v.tensor_copy(rsc[:, 4, 0:32], wnorm[:, tt, :]))
                        for k8 in range(8):
                            fence(v.reduce_max(rsc[:, 3, tt:tt + 1], rsc[:, 4, 0:32],
                                               axis=mybir.AxisListType.X))
                            if k8 < 7:
                                fence(v.tensor_scalar(rsc[:, 5, 0:32], rsc[:, 4, 0:32],
                                                      rsc[:, 3, tt:tt + 1], None, ALU.is_equal))
                                fence(v.scalar_tensor_tensor(rsc[:, 4, 0:32], rsc[:, 5, 0:32],
                                                             -1e30, rsc[:, 4, 0:32],
                                                             ALU.mult, ALU.add))
                        fence(v.tensor_scalar(rsc[:, 5, 0:32], wnorm[:, tt, :],
                                              rsc[:, 3, tt:tt + 1], None, ALU.is_ge))
                        fence(v.tensor_tensor(wnorm[:, tt, :], wnorm[:, tt, :], rsc[:, 5, 0:32],
                                              ALU.mult))
                        fence(v.reduce_sum(rsc[:, 3, tt:tt + 1], wnorm[:, tt, :],
                                           axis=mybir.AxisListType.X))
                        fence(v.tensor_scalar_max(rsc[:, 3, tt:tt + 1], rsc[:, 3, tt:tt + 1],
                                                  1e-9))
                        fence(v.reciprocal(rsc[:, 3, tt:tt + 1], rsc[:, 3, tt:tt + 1]))
                        v.tensor_scalar_mul(wnorm[:, tt, :], wnorm[:, tt, :],
                                            rsc[:, 3, tt:tt + 1]).then_inc(rdone.h, 1)
                        rdone.v += 1
                    for e in range(E):
                        for i in range(6):
                            for h2 in range(2):
                                gi = 2 * (e * 6 + i) + h2
                                v.wait_ge(sil.h, gi + 1)
                                v.wait_ge(umm.h, gi + 1)
                                if e >= 2 and i == 0 and h2 == 0:
                                    v.wait_ge(dmm.h, 32 * (e - 1))
                                v.tensor_tensor(hbuf[:, e % 2, i, h2 * 512:(h2 + 1) * 512],
                                                sg[:, h2, :], ps_u[:, h2, :], ALU.mult) \
                                    .then_inc(hh.h, 1)
                                hh.v += 1
                        for nch in range(4):
                            for tt in range(8):
                                di = e * 32 + nch * 8 + tt
                                v.wait_ge(dmm.h, di + 1)
                                v.scalar_tensor_tensor(
                                    xacc[:, tt, nch * 512:(nch + 1) * 512],
                                    ps_dn[:, di % 2, :], wnorm[:, tt, e:e + 1],
                                    xacc[:, tt, nch * 512:(nch + 1) * 512],
                                    ALU.mult, ALU.add).then_inc(ev.h, 1)
                                ev.v += 1

    def emit_attn(tag, mode, yT_all, krot=None, kvt=None, qrot=None,
                  dK=None, dV=None, dq=None):
        UNITS = []
        for h in range(NH):
            for p in range(4):
                for kt in range(8 * (p + 1)):
                    UNITS.append((h, p, kt))
        NPH = 80

        with ExitStack() as ctx:
            mask = ctx.enter_context(nc.sbuf_tensor(tag + "_mask", [128, NPH, 256], BF16))
            qps = ctx.enter_context(nc.sbuf_tensor(tag + "_qps", [128, NOWN], F32))
            probs = ctx.enter_context(nc.sbuf_tensor(tag + "_pr", [128, 2, 256], BF16))
            rb = ctx.enter_context(nc.sbuf_tensor(tag + "_rb", [128, 256], F32))
            rzb = ctx.enter_context(nc.sbuf_tensor(tag + "_rz", [1, 256], BF16))
            zt = ctx.enter_context(nc.sbuf_tensor(tag + "_zt", [1, 256], F32))
            if mode == "b":
                kT = ctx.enter_context(nc.sbuf_tensor(tag + "_kT", [128, 2, T], BF16))
                vT = ctx.enter_context(nc.sbuf_tensor(tag + "_vT", [128, 2, 32, 128], BF16))
                qT = ctx.enter_context(nc.sbuf_tensor(tag + "_qT", [128, 2, NOWN], BF16))
            ps_s = ctx.enter_context(nc.psum_tensor(tag + "_ps", [128, 2, 256], F32))
            ps_o = ctx.enter_context(nc.psum_tensor(tag + "_po", [128, 256], F32))
            ps_z = ctx.enter_context(nc.psum_tensor(tag + "_pz", [1, 256], F32))
            ps_b = ctx.enter_context(nc.psum_tensor(tag + "_pb", [128, 256], F32))

            with nc.Block(tag) as blk:
                kl = S(tag + "_kl"); mk = S(tag + "_mk"); ql = S(tag + "_ql")
                smm = S(tag + "_smm"); sq = S(tag + "_sq"); pm = S(tag + "_pm")
                av = S(tag + "_av"); rcp = S(tag + "_rcp"); bc = S(tag + "_bc")
                yn = S(tag + "_yn")

                @blk.gpsimd
                def _(g):
                    ql.inc(g.dma_start(out=qps[:, :], in_=d_qpos[:, :]))
                    g.wait_ge(ql.h, ql.v)

                if mode == "b":
                    @blk.sync
                    def _(sy):
                        for h in range(NH):
                            if h >= 2:
                                sy.wait_ge(yn.h, 4 * (h - 1))
                            kl.inc(sy.dma_start(out=kT[:, h % 2, :],
                                                in_=dK[:, h * 128:(h + 1) * 128],
                                                transpose=True))
                            kl.inc(sy.dma_start(
                                out=vT[:, h % 2, :, :],
                                in_=dV[:, h * 128:(h + 1) * 128].rearrange(
                                    "(b p) d -> p b d", p=128)))
                            kl.inc(sy.dma_start(out=qT[:, h % 2, :],
                                                in_=dq[:, h * 128:(h + 1) * 128],
                                                transpose=True))

                @blk.tensor
                def _(t):
                    def lhs_score(h, kt):
                        if mode == "a":
                            return krot[:, kt * 128:(kt + 1) * 128]
                        return kT[:, h % 2, kt * 128:(kt + 1) * 128]

                    def rhs_score(h, p):
                        if mode == "a":
                            return qrot[:, h, p * 256:(p + 1) * 256]
                        return qT[:, h % 2, p * 256:(p + 1) * 256]

                    def lhs_av(h, kt):
                        if mode == "a":
                            return kvt[:, kt, :]
                        return vT[:, h % 2, kt, :]

                    def emit_av(idx):
                        h2, p2, kt2 = UNITS[idx]
                        XKT2 = 8 * (p2 + 1)
                        t.wait_ge(pm.h, idx + 1)
                        hp2 = h2 * 4 + p2
                        if kt2 == 0 and hp2 >= 1:
                            t.wait_ge(yn.h, hp2)
                            t.wait_ge(rcp.h, hp2)
                        t.matmul(ps_o[:, :], lhs_av(h2, kt2), probs[:, idx % 2, :],
                                 start=(kt2 == 0), stop=(kt2 == XKT2 - 1))
                        ins = t.matmul(ps_z[:, :], sb_onescol[:, :], probs[:, idx % 2, :],
                                       start=(kt2 == 0), stop=(kt2 == XKT2 - 1))
                        ins.then_inc(av.h, 1); av.v += 1
                        if kt2 == XKT2 - 1:
                            t.wait_ge(rcp.h, hp2 + 1)
                            if hp2 >= 1:
                                t.wait_ge(yn.h, hp2)
                            t.matmul(ps_b[:, :], sb_onesrow[:, :], rzb[:, :],
                                     start=True, stop=True).then_inc(bc.h, 1)
                            bc.v += 1

                    for idx, (h, p, kt) in enumerate(UNITS):
                        if mode == "b" and kt == 0 and p == 0:
                            t.wait_ge(kl.h, 48 * (h + 1))
                        if idx >= 2:
                            t.wait_ge(sq.h, idx - 1)
                        t.matmul(ps_s[:, idx % 2, :], lhs_score(h, kt), rhs_score(h, p),
                                 start=True, stop=True).then_inc(smm.h, 1)
                        smm.v += 1
                        if idx >= 1:
                            emit_av(idx - 1)
                    emit_av(len(UNITS) - 1)

                @blk.scalar
                def _(sc):
                    for idx in range(len(UNITS)):
                        sc.wait_ge(smm.h, idx + 1)
                        if idx >= 2:
                            sc.wait_ge(av.h, idx - 1)
                        if mode == "a":
                            sc.activation(probs[:, idx % 2, :], ps_s[:, idx % 2, :],
                                          AF.Exp, bias=sb_eshift[:, 0:1]).then_inc(sq.h, 1)
                        else:
                            sc.activation(probs[:, idx % 2, :], ps_s[:, idx % 2, :],
                                          AF.Square).then_inc(sq.h, 1)
                        sq.v += 1

                @blk.vector
                def _(v):
                    vfp = S(tag + "_vf")
                    v.wait_ge(ql.h, 16)
                    u = 0
                    for p in range(4):
                        for kt in range(8 * (p + 1)):
                            ins = v.tensor_scalar(mask[:, u, :],
                                                  qps[:, p * 256:(p + 1) * 256],
                                                  sb_kpos[:, kt:kt + 1], None, ALU.is_ge)
                            u += 1
                    ins.then_inc(mk.h, 1); mk.v += 1
                    for h in range(NH):
                        u = 0
                        for p in range(4):
                            XKT = 8 * (p + 1)
                            for kt in range(XKT):
                                idx = h * NPH + u
                                v.wait_ge(sq.h, idx + 1)
                                v.tensor_tensor(probs[:, idx % 2, :], probs[:, idx % 2, :],
                                                mask[:, u, :], ALU.mult).then_inc(pm.h, 1)
                                pm.v += 1
                                u += 1
                            hp = h * 4 + p
                            v.wait_ge(av.h, h * NPH + sum(8 * (pp + 1) for pp in range(p + 1)))
                            if mode == "b":
                                v.tensor_scalar_add(zt[:, :], ps_z[:, :], EPS) \
                                    .then_inc(vfp.h, 1)
                                vfp.v += 1
                                v.wait_ge(vfp.h, vfp.v)
                                zsrc = zt
                            else:
                                zsrc = ps_z
                            with nc.allow_low_precision(reason="bf16 bcast operand"):
                                v.reciprocal(rzb[:, :], zsrc[:, :]).then_inc(rcp.h, 1)
                            rcp.v += 1
                            v.wait_ge(bc.h, hp + 1)
                            v.tensor_copy(rb[:, :], ps_b[:, :]).then_inc(vfp.h, 1)
                            vfp.v += 1
                            v.wait_ge(vfp.h, vfp.v)
                            v.tensor_tensor(yT_all[:, h, p * 256:(p + 1) * 256], ps_o[:, :],
                                            rb[:, :], ALU.mult).then_inc(yn.h, 1)
                            yn.v += 1

    def emit_proj(tag, yT, w_dram, bias_sb, resid_dram, out_dram, extra_out=None):
        with ExitStack() as ctx:
            wo = ctx.enter_context(nc.sbuf_tensor(tag + "_wo", [128, 2, C], BF16))
            x2t = ctx.enter_context(nc.sbuf_tensor(tag + "_x2", [128, 2, C], F32))
            x3t = ctx.enter_context(nc.sbuf_tensor(tag + "_x3", [128, 2, C], F32))
            ps_o2 = ctx.enter_context(nc.psum_tensor(tag + "_po2", [128, 2, C], F32))
            with nc.Block(tag) as blk:
                wol = S(tag + "_wol"); x2l = S(tag + "_x2l"); ommf = S(tag + "_ommf")
                res = S(tag + "_res"); xst = S(tag + "_xst")

                @blk.gpsimd
                def _(g):
                    for qtr in range(4):
                        for yf in range(16):
                            n = qtr * 16 + yf
                            if n >= 2:
                                g.wait_ge(ommf.h, n - 1)
                            wol.inc(g.dma_start(out=wo[:, n % 2, :],
                                                in_=w_dram[yf * 128:(yf + 1) * 128, :]))

                @blk.sync
                def _(sy):
                    for qtr in range(4):
                        if qtr >= 1:
                            sy.wait_ge(res.h, 2 * qtr)
                        for tt2 in range(2):
                            r0 = qtr * 256 + tt2 * 128
                            x2l.inc(sy.dma_start(out=x2t[:, tt2, :],
                                                 in_=resid_dram[r0:r0 + 128, :]))

                @blk.tensor
                def _(t):
                    for qtr in range(4):
                        if qtr >= 1:
                            t.wait_ge(res.h, 2 * qtr)
                        for yf in range(16):
                            n = qtr * 16 + yf
                            t.wait_ge(wol.h, 16 * (n + 1))
                            for tt2 in range(2):
                                col = qtr * 256 + tt2 * 128
                                for nch in range(4):
                                    ins = t.matmul(ps_o2[:, tt2, nch * 512:(nch + 1) * 512],
                                                   yT[:, yf, col:col + 128],
                                                   wo[:, n % 2, nch * 512:(nch + 1) * 512],
                                                   start=(yf == 0), stop=(yf == 15))
                            ins.then_inc(ommf.h, 1); ommf.v += 1

                @blk.vector
                def _(v):
                    for qtr in range(4):
                        v.wait_ge(ommf.h, 16 * (qtr + 1))
                        v.wait_ge(x2l.h, 16 * 2 * (qtr + 1))
                        for tt2 in range(2):
                            if qtr >= 1:
                                nst = 1 if extra_out is None else 2
                                v.wait_ge(xst.h, 16 * nst * (2 * (qtr - 1) + tt2 + 1))
                            ins = v.tensor_tensor(x3t[:, tt2, :], ps_o2[:, tt2, :],
                                                  x2t[:, tt2, :], ALU.add)
                            if bias_sb is not None:
                                ins = v.tensor_tensor(x3t[:, tt2, :], x3t[:, tt2, :],
                                                      bias_sb[:, :], ALU.add)
                            ins.then_inc(res.h, 1)
                            res.v += 1

                @blk.scalar
                def _(sc):
                    for qtr in range(4):
                        for tt2 in range(2):
                            nidx = 2 * qtr + tt2
                            sc.wait_ge(res.h, nidx + 1)
                            r0 = qtr * 256 + tt2 * 128
                            xst.inc(sc.dma_start(out=out_dram[r0:r0 + 128, :],
                                                 in_=x3t[:, tt2, :]))
                            if extra_out is not None:
                                xst.inc(sc.dma_start(out=extra_out[r0:r0 + 128, :],
                                                     in_=x3t[:, tt2, :]))
                    sc.wait_ge(xst.h, xst.v)

    def emit_qkv(tag, xn_src, dst, fm, wT_dram, rstd_bias, sb_gam=None, sb_bet=None):
        ntiles = NOWN // 128
        with ExitStack() as ctx:
            xnT = ctx.enter_context(nc.sbuf_tensor(tag + "_xnT", [128, 2, 16, 512], BF16))
            wts = ctx.enter_context(nc.sbuf_tensor(tag + "_w", [128, 16, C], BF16))
            ot = ctx.enter_context(nc.sbuf_tensor(tag + "_ot", [128, 2, C], BF16))
            t1 = ctx.enter_context(nc.sbuf_tensor(tag + "_t1", [128, C], F32))
            stf = ctx.enter_context(nc.sbuf_tensor(tag + "_stf", [128, 6 * NH], F32))
            agf = ctx.enter_context(nc.sbuf_tensor(tag + "_agf", [128, 4 * NH], F32))
            ps_k = ctx.enter_context(nc.psum_tensor(tag + "_pk", [128, 2, C], F32))
            with nc.Block(tag) as blk:
                wld = S(tag + "_wld"); tl = S(tag + "_tl"); kmm = S(tag + "_kmm")
                fmv = S(tag + "_fmv"); fma = S(tag + "_fma"); fmd = S(tag + "_fmd")
                ost = S(tag + "_ost")

                @blk.gpsimd
                def _(g):
                    for xf in range(16):
                        wld.inc(g.dma_start(out=wts[:, xf, :],
                                            in_=wT_dram[xf * 128:(xf + 1) * 128, :]))

                @blk.sync
                def _(sy):
                    for bb in range(ntiles // 4):
                        if bb >= 2:
                            sy.wait_ge(kmm.h, 4 * (bb - 1))
                        for xf in range(16):
                            tl.inc(sy.dma_start(
                                out=xnT[:, bb % 2, xf, :],
                                in_=xn_src[bb * 512:(bb + 1) * 512, xf * 128:(xf + 1) * 128],
                                transpose=True))

                @blk.tensor
                def _(t):
                    t.wait_ge(wld.h, wld.v)
                    n = 0
                    for bb in range(ntiles // 4):
                        t.wait_ge(tl.h, 16 * 16 * (bb + 1))
                        for t4 in range(4):
                            if n >= 2:
                                t.wait_ge(fmd.h, n - 1)
                            for xf in range(16):
                                for nch in range(4):
                                    ins = t.matmul(
                                        ps_k[:, n % 2, nch * 512:(nch + 1) * 512],
                                        xnT[:, bb % 2, xf, t4 * 128:(t4 + 1) * 128],
                                        wts[:, xf, nch * 512:(nch + 1) * 512],
                                        start=(xf == 0), stop=(xf == 15))
                            ins.then_inc(kmm.h, 1); kmm.v += 1
                            n += 1

                @blk.vector
                def _(v):
                    vf = S(tag + "_vf")
                    for n in range(ntiles):
                        v.wait_ge(kmm.h, n + 1)
                        p = n % 2
                        if fm:
                            v.tensor_tensor(t1[:, :], ps_k[:, p, :], sb_gam[:, :], ALU.mult)
                            v.tensor_tensor(t1[:, :], t1[:, :], sb_bet[:, :], ALU.add)
                            for hh2 in range(NH):
                                ins = v.bn_stats(stf[:, hh2 * 6:(hh2 + 1) * 6],
                                                 t1[:, hh2 * 128:(hh2 + 1) * 128])
                            ins.then_inc(vf.h, 1); vf.v += 1
                            v.wait_ge(vf.h, vf.v)
                            for hh2 in range(NH):
                                ins = v.bn_aggr(agf[:, hh2 * 2:hh2 * 2 + 2],
                                                stf[:, hh2 * 6:(hh2 + 1) * 6])
                            ins.then_inc(vf.h, 1); vf.v += 1
                            v.wait_ge(vf.h, vf.v)
                            for hh2 in range(NH):
                                ins = v.tensor_scalar_add(
                                    agf[:, 2 * NH + hh2:2 * NH + hh2 + 1],
                                    agf[:, hh2 * 2 + 1:hh2 * 2 + 2], EPS)
                            ins.then_inc(fmv.h, 1)
                            fmv.v += 1
                            v.wait_ge(fma.h, fmv.v)
                            if n >= 2:
                                v.wait_ge(ost.h, 16 * (n - 1))
                            for hh2 in range(NH):
                                ins = v.tensor_scalar(
                                    ot[:, p, hh2 * 128:(hh2 + 1) * 128],
                                    t1[:, hh2 * 128:(hh2 + 1) * 128],
                                    agf[:, hh2 * 2:hh2 * 2 + 1],
                                    agf[:, 3 * NH + hh2:3 * NH + hh2 + 1],
                                    ALU.subtract, ALU.mult)
                            ins.then_inc(fmd.h, 1)
                            fmd.v += 1
                        else:
                            if n >= 2:
                                v.wait_ge(ost.h, 16 * (n - 1))
                            v.tensor_copy(ot[:, p, :], ps_k[:, p, :]).then_inc(fmd.h, 1)
                            fmd.v += 1

                @blk.scalar
                def _(sc):
                    if fm:
                        sf = S(tag + "_sf")
                        for n in range(ntiles):
                            sc.wait_ge(fmv.h, n + 1)
                            bias_ap = sb_zero[:, 0:1] if rstd_bias == 0.0 \
                                else sb_lnhd[:, 0:1]
                            ins = sc.activation(agf[:, 3 * NH:4 * NH],
                                                agf[:, 2 * NH:3 * NH], AF.Ln)
                            ins.then_inc(sf.h, 1); sf.v += 1
                            sc.wait_ge(sf.h, sf.v)
                            ins = sc.activation(agf[:, 3 * NH:4 * NH],
                                                agf[:, 3 * NH:4 * NH], AF.Exp,
                                                scale=-0.5, bias=bias_ap)
                            ins.then_inc(fma.h, 1)
                            fma.v += 1

                @blk.gpsimd
                def _(g):
                    for n in range(ntiles):
                        g.wait_ge(fmd.h, n + 1)
                        ost.inc(g.dma_start(out=dst[n * 128:(n + 1) * 128, :],
                                            in_=ot[:, n % 2, :]))
                    g.wait_ge(ost.h, ost.v)


    with nc.Block("p0") as blk:
        ld = S("p0_ld"); cast = S("p0_cast")

        @blk.sync
        def _(sy):
            pairs = [(sb_kpos[:, :], d_kpos[:, :])]
            if hasA:
                pairs += [(sb_bd[:, :], d_bd[:, :]), (sb_blat[:, :], d_blat[:, :])]
            for dst, src in pairs:
                ld.inc(sy.dma_start(out=dst, in_=src))
            sy.wait_ge(ld.h, ld.v)

        @blk.gpsimd
        def _(g):
            for m in range(16):
                cast.inc(g.dma_start(out=sb_gwT[:, m, :],
                                     in_=d_gwT[m * 128:(m + 1) * 128, :]))
            cast.inc(g.dma_start(out=sb_onescol[:, :], in_=c_onescol[:, :]))
            cast.inc(g.dma_start(out=sb_onesrow[:, :], in_=c_onesrow[:, :]))
            if hasA:
                cast.inc(g.dma_start(out=sb_PT[:, :], in_=c_PT[:, :]))
                cast.inc(g.dma_start(out=sb_ident[:, :], in_=c_ident[:, :]))
            g.memset(sb_eshift[:, :], EXP_SHIFT).then_inc(cast.h, 1)
            g.memset(sb_zero[:, :], 0.0).then_inc(cast.h, 1)
            g.memset(sb_lnhd[:, :], LN_HD).then_inc(cast.h, 1)
            cast.v += 3
            g.wait_ge(cast.h, cast.v)

    free_sems()

    if hasA:
        emit_ln("p1", d_x_batch, 0, d_xn, 0, T // 128)
        free_sems()
        emit_ln("p1b", d_x_own, 0, d_xno, 0, NOWN // 128)
        free_sems()

        with ExitStack() as octx:
            sb_kvt = octx.enter_context(nc.sbuf_tensor("kv_t", [128, T // 128, LC], BF16))
            sb_krot = octx.enter_context(nc.sbuf_tensor("k_rot", [128, T], BF16))
            sb_bproj = octx.enter_context(nc.sbuf_tensor("bproj_sb", [128, C], F32))
            qrot = octx.enter_context(nc.sbuf_tensor("qrot", [128, NH, NOWN], BF16))
            yT = octx.enter_context(nc.sbuf_tensor("yT_a", [128, NH, NOWN], BF16))

            with ExitStack() as ctx:
                xnT = ctx.enter_context(nc.sbuf_tensor("p1c_xnT", [128, 2, 16, 512], BF16))
                wlat = ctx.enter_context(nc.sbuf_tensor("p1c_wlat", [128, 16, LC], BF16))
                kvb = ctx.enter_context(nc.sbuf_tensor("p1c_kvb", [128, 512], BF16))
                tmp = ctx.enter_context(nc.sbuf_tensor("p1c_tmp", [128, 512], F32))
                cosk = ctx.enter_context(nc.sbuf_tensor("p1c_cosk", [128, T], F32))
                sink = ctx.enter_context(nc.sbuf_tensor("p1c_sink", [128, T], F32))
                ps_kv = ctx.enter_context(nc.psum_tensor("p1c_ps", [128, 512], F32))
                ps_rot = ctx.enter_context(nc.psum_tensor("p1c_psr", [128, 512], F32))
                ps_tr = ctx.enter_context(nc.psum_tensor("p1c_pst", [128, 512], BF16))
                NBLK = T // 512
                with nc.Block("p1c") as blk:
                    wl = S("p1c_wl"); tl = S("p1c_tl"); mm = S("p1c_mm")
                    bias = S("p1c_bias"); rot = S("p1c_rot"); mix = S("p1c_mix")
                    tr = S("p1c_tr"); evt = S("p1c_evt")

                    @blk.gpsimd
                    def _(g):
                        wl.inc(g.dma_start(out=wlat[:, :, :],
                                           in_=d_Wlat.rearrange("(t p) l -> p t l", p=128)))
                        wl.inc(g.dma_start(out=cosk[:, :], in_=d_cosk[:, :]))
                        wl.inc(g.dma_start(out=sink[:, :], in_=d_sink[:, :]))
                        wl.inc(g.dma_start(out=sb_bproj[:, :], in_=d_bproj[:, :]))

                    @blk.sync
                    def _(sy):
                        for bb in range(NBLK):
                            if bb >= 2:
                                sy.wait_ge(mm.h, bb - 1)
                            for xf in range(16):
                                tl.inc(sy.dma_start(
                                    out=xnT[:, bb % 2, xf, :],
                                    in_=d_xn[bb * 512:(bb + 1) * 512,
                                             xf * 128:(xf + 1) * 128],
                                    transpose=True))

                    @blk.tensor
                    def _(t):
                        t.wait_ge(wl.h, wl.v)
                        for bb in range(NBLK):
                            t.wait_ge(tl.h, 16 * 16 * (bb + 1))
                            if bb >= 1:
                                t.wait_ge(bias.h, bb)
                            for xf in range(16):
                                ins = t.matmul(ps_kv[:, :], wlat[:, xf, :],
                                               xnT[:, bb % 2, xf, :],
                                               start=(xf == 0), stop=(xf == 15))
                            ins.then_inc(mm.h, 1); mm.v += 1
                            t.wait_ge(bias.h, bb + 1)
                            if bb >= 1:
                                t.wait_ge(mix.h, bb)
                            t.matmul(ps_rot[:, :], sb_PT[:, :], kvb[:, :],
                                     start=True, stop=True).then_inc(rot.h, 1)
                            rot.v += 1
                            if bb >= 1:
                                t.wait_ge(evt.h, bb)
                            for q in range(4):
                                ins = t.transpose(ps_tr[:, q * 128:(q + 1) * 128],
                                                  kvb[:, q * 128:(q + 1) * 128],
                                                  sb_ident[:, :])
                            ins.then_inc(tr.h, 1); tr.v += 1

                    @blk.vector
                    def _(v):
                        for bb in range(NBLK):
                            v.wait_ge(mm.h, bb + 1)
                            if bb >= 1:
                                v.wait_ge(tr.h, bb)
                            v.tensor_scalar_add(kvb[:, :], ps_kv[:, :], sb_blat[:, 0:1]) \
                                .then_inc(bias.h, 1)
                            bias.v += 1
                            v.wait_ge(rot.h, bb + 1)
                            sl = slice(bb * 512, (bb + 1) * 512)
                            v.tensor_tensor(tmp[:, :], ps_rot[:, :], sink[:, sl], ALU.mult)
                            v.tensor_tensor(sb_krot[:, sl], kvb[:, :], cosk[:, sl], ALU.mult)
                            v.tensor_tensor(sb_krot[:, sl], sb_krot[:, sl], tmp[:, :],
                                            ALU.add).then_inc(mix.h, 1)
                            mix.v += 1
                            v.wait_ge(tr.h, bb + 1)
                            for q in range(4):
                                ins = v.tensor_copy(sb_kvt[:, bb * 4 + q, :],
                                                    ps_tr[:, q * 128:(q + 1) * 128])
                            ins.then_inc(evt.h, 1); evt.v += 1

            free_sems()

            with ExitStack() as ctx:
                cosq = ctx.enter_context(nc.sbuf_tensor("p2_cosq", [128, NOWN], F32))
                sinq = ctx.enter_context(nc.sbuf_tensor("p2_sinq", [128, NOWN], F32))
                xnTo = ctx.enter_context(nc.sbuf_tensor("p2_xnTo", [128, 2, 16, 512], BF16))
                wd = ctx.enter_context(nc.sbuf_tensor("p2_wd", [128, 2, 16, 128], BF16))
                qbt = ctx.enter_context(nc.sbuf_tensor("p2_qbt", [128, 512], BF16))
                tmp2 = ctx.enter_context(nc.sbuf_tensor("p2_tmp", [128, 512], F32))
                ps_q = ctx.enter_context(nc.psum_tensor("p2_psq", [128, 512], F32))
                ps_r = ctx.enter_context(nc.psum_tensor("p2_psr", [128, 512], F32))
                NJ = NOWN // 512
                with nc.Block("p2q") as blk:
                    cl = S("p2_cl"); xl = S("p2_xl"); wlo = S("p2_wl"); qmm = S("p2_qmm")
                    qb_s = S("p2_qb"); qrt = S("p2_qrt"); qmx = S("p2_qmx")

                    @blk.gpsimd
                    def _(g):
                        cl.inc(g.dma_start(out=cosq[:, :], in_=d_cosq[:, :]))
                        cl.inc(g.dma_start(out=sinq[:, :], in_=d_sinq[:, :]))
                        n = 0
                        for j in range(NJ):
                            for qf in range(16):
                                if n >= 2:
                                    g.wait_ge(qmm.h, n - 1)
                                wlo.inc(g.dma_start(
                                    out=wd[:, n % 2, :, :],
                                    in_=d_Wd[:, qf * 128:(qf + 1) * 128].rearrange(
                                        "(t p) q -> p t q", p=128)))
                                n += 1

                    @blk.sync
                    def _(sy):
                        for j in range(NJ):
                            if j >= 2:
                                sy.wait_ge(qmm.h, 16 * (j - 1))
                            for xf in range(16):
                                xl.inc(sy.dma_start(
                                    out=xnTo[:, j % 2, xf, :],
                                    in_=d_xno[j * 512:(j + 1) * 512,
                                              xf * 128:(xf + 1) * 128],
                                    transpose=True))

                    @blk.tensor
                    def _(t):
                        n = 0
                        for j in range(NJ):
                            t.wait_ge(xl.h, 16 * 16 * (j + 1))
                            for qf in range(16):
                                t.wait_ge(wlo.h, 16 * (n + 1))
                                if n >= 1:
                                    t.wait_ge(qb_s.h, n)
                                for xf in range(16):
                                    ins = t.matmul(ps_q[:, :], wd[:, n % 2, xf, :],
                                                   xnTo[:, j % 2, xf, :],
                                                   start=(xf == 0), stop=(xf == 15))
                                ins.then_inc(qmm.h, 1); qmm.v += 1
                                t.wait_ge(qb_s.h, n + 1)
                                if n >= 1:
                                    t.wait_ge(qmx.h, n)
                                t.matmul(ps_r[:, :], sb_PT[:, :], qbt[:, :],
                                         start=True, stop=True).then_inc(qrt.h, 1)
                                qrt.v += 1
                                n += 1

                    @blk.vector
                    def _(v):
                        v.wait_ge(cl.h, cl.v)
                        n = 0
                        for j in range(NJ):
                            sl = slice(j * 512, (j + 1) * 512)
                            for qf in range(16):
                                v.wait_ge(qmm.h, n + 1)
                                if n >= 1:
                                    v.wait_ge(qrt.h, n)
                                v.tensor_scalar_add(qbt[:, :], ps_q[:, :],
                                                    sb_bd[:, qf:qf + 1]).then_inc(qb_s.h, 1)
                                qb_s.v += 1
                                v.wait_ge(qrt.h, n + 1)
                                v.tensor_tensor(tmp2[:, :], ps_r[:, :], sinq[:, sl], ALU.mult)
                                v.tensor_tensor(qrot[:, qf, sl], qbt[:, :], cosq[:, sl],
                                                ALU.mult)
                                v.tensor_tensor(qrot[:, qf, sl], qrot[:, qf, sl], tmp2[:, :],
                                                ALU.add).then_inc(qmx.h, 1)
                                qmx.v += 1
                                n += 1

            free_sems()

            emit_attn("p2a", "a", yT, krot=sb_krot, kvt=sb_kvt, qrot=qrot)
            free_sems()

            emit_proj("p2j", yT, d_Wproj, sb_bproj, d_x_own, d_xr)
            free_sems()

        with ExitStack() as ctx:
            xacc = ctx.enter_context(nc.sbuf_tensor("p3x", [128, 8, C], F32))
            emit_ln("p3l", d_xr, 0, d_xns, 0, 8, xacc=xacc)
            emit_moe("p3m", xacc, d_xns, 0)
            with nc.Block("p3s") as blk:
                xst = S("p3s_xst")

                @blk.gpsimd
                def _(g):
                    for tt in range(8):
                        xst.inc(g.dma_start(out=d_x2[tt * 128:(tt + 1) * 128, :],
                                            in_=xacc[:, tt, :]))
                    g.wait_ge(xst.h, xst.v)
        free_sems()

        with ExitStack() as p4ctx:
            sb_gam = p4ctx.enter_context(nc.sbuf_tensor("gam_sb", [128, C], F32))
            sb_bet = p4ctx.enter_context(nc.sbuf_tensor("bet_sb", [128, C], F32))
            with nc.Block("p4g") as blk:
                gbl = S("p4g_gbl")

                @blk.sync
                def _(sy):
                    gbl.inc(sy.dma_start(out=sb_gam[:, :], in_=d_gamma[:, :]))
                    gbl.inc(sy.dma_start(out=sb_bet[:, :], in_=d_beta[:, :]))
                    sy.wait_ge(gbl.h, gbl.v)

            emit_ln("p4l", d_x2, 0, d_xns2, 0, 8)
            free_sems()
            emit_qkv("p4k", d_xns2, d_K, True, d_WkT, 0.0, sb_gam, sb_bet)
            free_sems()
            emit_qkv("p4v", d_xns2, d_V, False, d_WvT, 0.0)
            free_sems()
            emit_qkv("p4q", d_xns2, d_q, True, d_WqT, LN_HD, sb_gam, sb_bet)
            free_sems()

    if hasB:
        with ExitStack() as octx:
            oT = octx.enter_context(nc.sbuf_tensor("oT", [128, NH, NOWN], BF16))
            emit_attn("p5a", "b", oT, dK=d_Kf, dV=d_Vf, dq=d_qf)
            free_sems()
            emit_proj("p5j", oT, d_WoT, None, d_x2in, d_x3)
            free_sems()

        with ExitStack() as ctx:
            xacc = ctx.enter_context(nc.sbuf_tensor("p6x", [128, 8, C], F32))
            emit_ln("p6l", d_x3, 0, d_xns, 0, 8, xacc=xacc)
            emit_moe("p6m", xacc, d_xns, 0)
            with nc.Block("p6s") as blk:
                xst = S("p6s_xst")

                @blk.gpsimd
                def _(g):
                    for tt in range(8):
                        xst.inc(g.dma_start(out=d_out[tt * 128:(tt + 1) * 128, :],
                                            in_=xacc[:, tt, :]))
                    g.wait_ge(xst.h, xst.v)
        free_sems()

    return nc



def _rows(sub):
    return np.concatenate([np.arange(c * 128, (c + 1) * 128) for c in OWN[sub]])


def _stage_A(inputs):
    x = np.asarray(inputs["L0_attn_a_x"])
    cosf, sinf = _rope_tables(T)
    scale = np.float32(1.0 / np.sqrt(LC))
    kposm = np.ascontiguousarray(np.arange(T, dtype=np.float32).reshape(T // 128, 128).T)
    WqT = np.ascontiguousarray(np.asarray(inputs["L0_attn_b_Wq"]).T)
    WkT = np.ascontiguousarray(np.asarray(inputs["L0_attn_b_Wk"]).T)
    WvT = np.ascontiguousarray(np.asarray(inputs["L0_attn_b_Wv"]).T)
    gwT = np.ascontiguousarray(np.asarray(inputs["L0_ffn_gate_w"]).T)
    gamma = np.ascontiguousarray(np.broadcast_to(
        np.tile(np.asarray(inputs["L0_attn_b_gamma"]), NH).reshape(1, C), (128, C)))
    beta = np.ascontiguousarray(np.broadcast_to(
        np.tile(np.asarray(inputs["L0_attn_b_beta"]), NH).reshape(1, C), (128, C)))
    bproj = np.ascontiguousarray(np.broadcast_to(
        np.asarray(inputs["L0_attn_a_b_proj"]).reshape(1, C), (128, C)))
    bd = np.ascontiguousarray(np.asarray(inputs["L0_attn_a_b_d"]).reshape(NH, LC).T)
    blat = np.ascontiguousarray(np.asarray(inputs["L0_attn_a_b_latent"]).reshape(LC, 1))
    wg = np.ascontiguousarray(inputs["L0_ffn_w_gate"])
    wu = np.ascontiguousarray(inputs["L0_ffn_w_up"])
    wdn = np.ascontiguousarray(inputs["L0_ffn_w_down"])
    Wlat = np.ascontiguousarray(inputs["L0_attn_a_W_latent"])
    Wd = np.ascontiguousarray(inputs["L0_attn_a_W_d"])
    Wproj = np.ascontiguousarray(inputs["L0_attn_a_W_proj"])

    in_maps = []
    for core in range(8):
        b, sub = core // 4, core % 4
        rows = _rows(sub)
        m = {
            "x_own": np.ascontiguousarray(x[b][rows]),
            "x_batch": np.ascontiguousarray(x[b]),
            "Wlat": Wlat, "blat": blat, "Wd": Wd, "bd": bd,
            "Wproj": Wproj, "bproj": bproj,
            "WqT": WqT, "WkT": WkT, "WvT": WvT,
            "gamma": gamma, "beta": beta,
            "gwT": gwT, "wg": wg, "wu": wu, "wdn": wdn,
            "cosq": np.ascontiguousarray(cosf[:, rows] * scale),
            "sinq": np.ascontiguousarray(sinf[:, rows] * scale),
            "cosk": cosf, "sink": sinf,
            "qpos": np.ascontiguousarray(np.broadcast_to(
                rows.astype(np.float32).reshape(1, NOWN), (128, NOWN))),
            "kpos": kposm,
        }
        in_maps.append(m)
    return in_maps


def _stage_B(inputs, resA):
    import ml_dtypes
    WoT = np.ascontiguousarray(np.asarray(inputs["L0_attn_b_Wo"]).T)
    gwT = np.ascontiguousarray(np.asarray(inputs["L0_ffn_gate_w"]).T)
    wg = np.ascontiguousarray(inputs["L0_ffn_w_gate"])
    wu = np.ascontiguousarray(inputs["L0_ffn_w_up"])
    wdn = np.ascontiguousarray(inputs["L0_ffn_w_down"])
    kposm = np.ascontiguousarray(np.arange(T, dtype=np.float32).reshape(T // 128, 128).T)

    Kf = [np.empty((T, C), dtype=ml_dtypes.bfloat16) for _ in range(B)]
    Vf = [np.empty((T, C), dtype=ml_dtypes.bfloat16) for _ in range(B)]
    for core in range(8):
        b, sub = core // 4, core % 4
        rows = _rows(sub)
        Kf[b][rows] = resA[core]["K"]
        Vf[b][rows] = resA[core]["V"]

    in_maps = []
    for core in range(8):
        b, sub = core // 4, core % 4
        rows = _rows(sub)
        m = {
            "x2in": np.array(resA[core]["x2"], dtype=np.float32, copy=True),
            "Kf": Kf[b], "Vf": Vf[b],
            "qf": np.array(resA[core]["q"], copy=True),
            "WoT": WoT,
            "gwT": gwT, "wg": wg, "wu": wu, "wdn": wdn,
            "qpos": np.ascontiguousarray(np.broadcast_to(
                rows.astype(np.float32).reshape(1, NOWN), (128, NOWN))),
            "kpos": kposm,
        }
        in_maps.append(m)
    return in_maps


_NC_CACHE = {}


def kernel(**inputs):
    import ml_dtypes
    if "AB" not in _NC_CACHE:
        _NC_CACHE["AB"] = build_program("AB")
    ncAB = _NC_CACHE["AB"]

    maps1 = _stage_A(inputs)
    WoT = np.ascontiguousarray(np.asarray(inputs["L0_attn_b_Wo"]).T)
    zK = np.zeros((T, C), dtype=ml_dtypes.bfloat16)
    zq = np.zeros((NOWN, C), dtype=ml_dtypes.bfloat16)
    zx = np.zeros((NOWN, C), dtype=np.float32)
    for m in maps1:
        m.update({"x2in": zx, "Kf": zK, "Vf": zK, "qf": zq, "WoT": WoT})

    resA = run_bass_kernel_spmd(ncAB, maps1, core_ids=list(range(8))).results
    resA = [{k: np.array(v, copy=True) for k, v in r.items()} for r in resA]

    Kf = [np.empty((T, C), dtype=ml_dtypes.bfloat16) for _ in range(B)]
    Vf = [np.empty((T, C), dtype=ml_dtypes.bfloat16) for _ in range(B)]
    for core in range(8):
        b, sub = core // 4, core % 4
        rows = _rows(sub)
        Kf[b][rows] = resA[core]["K"]
        Vf[b][rows] = resA[core]["V"]

    maps2 = [dict(m) for m in maps1]
    for core in range(8):
        b = core // 4
        maps2[core].update({
            "x2in": np.array(resA[core]["x2"], dtype=np.float32, copy=True),
            "Kf": Kf[b], "Vf": Vf[b],
            "qf": np.array(resA[core]["q"], copy=True),
        })

    resB = run_bass_kernel_spmd(ncAB, maps2, core_ids=list(range(8))).results

    out = np.zeros((B, T, C), dtype=np.float32)
    for core in range(8):
        b, sub = core // 4, core % 4
        out[b][_rows(sub)] = resB[core]["out"]
    return out
